# version 22
# baseline (speedup 1.0000x reference)
import os
import numpy as np
from contextlib import ExitStack

import concourse.bass as bass
import concourse.mybir as mybir
from concourse.bass_utils import run_bass_kernel_spmd

F32 = mybir.dt.float32
BF16 = mybir.dt.bfloat16
AF = mybir.ActivationFunctionType
ALU = mybir.AluOpType

ENGS = ("sy", "pe", "ac", "ve", "gp")
OWN = {"pe": "pe", "ac": None, "ve": None, "gp": None, "sy": None}
CHAINED = ("ve", "ac", "gp")

D = 1024
NCH = 8
H = 16
HD = 64
W = 128
WPS = 32
STW = 4
EPS = 1e-5
DEBUG_BARRIER = False
PHASE = int(os.environ.get("ATH_PHASE", "9"))
SUB = int(os.environ.get("ATH_SUB", "9"))


class KB:

    def __init__(self):
        self.ops = {e: [] for e in ENGS}
        self.cnt = {"pe": 0, "ac": 0, "ve": 0, "gp": 0,
                    "di0": 0, "di1": 0, "dx0": 0, "dx1": 0, "dc": 0,
                    "do0": 0, "do1": 0, "dt0": 0, "dt1": 0}
        self.hwm = {}

    def emit(self, eng, fn, waits=(), inc=None, n=1):
        ws = []
        seen = set()
        wl = list(waits)
        if eng in CHAINED:
            if inc is None:
                inc, n = eng, 1
            prev = self.cnt[eng]
            if prev > 0:
                wl = [(eng, prev)] + wl
        if DEBUG_BARRIER:
            wl = wl + [(s, v) for s, v in self.cnt.items() if v > 0]
        for t in wl:
            if t is None:
                continue
            s, v = t
            if v <= 0 or s == OWN[eng] or s in seen:
                if s in seen:
                    ws = [(s0, max(v0, v) if s0 == s else v0) for s0, v0 in ws]
                continue
            if self.hwm.get((eng, s), 0) >= v:
                continue
            self.hwm[(eng, s)] = v
            seen.add(s)
            ws.append((s, v))
        self.ops[eng].append((fn, ws, inc, n))
        if inc:
            self.cnt[inc] += n
            return (inc, self.cnt[inc])
        return None

    def group(self, eng, fns, waits=(), inc=None, n=1):
        tok = None
        for i, fn in enumerate(fns):
            tok = self.emit(eng, fn, waits if i == 0 else (),
                            inc if i == len(fns) - 1 else None, n)
        return tok


class Res:

    def __init__(self):
        self.wtok = None
        self.rtoks = []

    def rd(self):
        return [self.wtok] if self.wtok else []

    def wr(self):
        out = list(self.rtoks)
        if self.wtok:
            out.append(self.wtok)
        return out

    def did_rd(self, tok):
        assert tok is not None
        self.rtoks.append(tok)

    def did_wr(self, tok):
        assert tok is not None
        self.wtok = tok
        self.rtoks = []


def build_program(n_windows=64):
    assert n_windows % STW == 0
    NW = n_windows
    NST = NW // STW
    TOK = NW * W

    nc = bass.Bass("TRN2", target_bir_lowering=False)

    x_h = nc.dram_tensor("x", [TOK, D], F32, kind="ExternalInput").ap()
    wq_h = nc.dram_tensor("Wq", [D, D], F32, kind="ExternalInput").ap()
    wk_h = nc.dram_tensor("Wk", [D, D], F32, kind="ExternalInput").ap()
    wv_h = nc.dram_tensor("Wv", [D, D], F32, kind="ExternalInput").ap()
    wo_h = nc.dram_tensor("Wo", [D, D], F32, kind="ExternalInput").ap()
    wf_h = nc.dram_tensor("Wf", [D, D], F32, kind="ExternalInput").ap()
    bq_h = nc.dram_tensor("bq", [D], F32, kind="ExternalInput").ap()
    bk_h = nc.dram_tensor("bk", [D], F32, kind="ExternalInput").ap()
    bv_h = nc.dram_tensor("bv", [D], F32, kind="ExternalInput").ap()
    bo_h = nc.dram_tensor("bo", [D], F32, kind="ExternalInput").ap()
    bf_h = nc.dram_tensor("bf", [D], F32, kind="ExternalInput").ap()
    vr_h = nc.dram_tensor("Vr", [D, 32], F32, kind="ExternalInput").ap()
    cm_h = nc.dram_tensor("Cm", [D, 32], F32, kind="ExternalInput").ap()
    out_h = nc.dram_tensor("out", [TOK, D], F32, kind="ExternalOutput").ap()

    kb = KB()
    ctx = ExitStack()
    S = lambda *a: ctx.enter_context(nc.sbuf_tensor(*a))
    P = lambda *a: ctx.enter_context(nc.psum_tensor(*a))

    wq = S("wq", [128, NCH, D], BF16)
    wk = S("wk", [128, NCH, D], BF16)
    wv = S("wv", [128, NCH, D], BF16)
    wo = S("wo", [128, NCH, D], BF16)
    wt = S("wt", [128, NCH, D], BF16)
    vr = S("vr", [128, NCH, 32], BF16)
    ct = S("ct", [32, D], BF16)
    bqc = S("bqc", [128, NCH], F32)
    bkc = S("bkc", [128, NCH], F32)
    bvc = S("bvc", [128, NCH], F32)
    bvb = S("bvb", [128, NCH], BF16)
    bcomb = S("bcomb", [1, D], BF16)
    bffn = S("bffn", [1, D], BF16)
    bcomb_bc = S("bcomb_bc", [128, D], BF16)
    bffn_bc = S("bffn_bc", [128, D], BF16)
    ident = S("ident", [128, 128], BF16)
    m01 = S("m01", [128, 128], BF16)
    ones_r = S("ones_r", [1, 128], BF16)
    ones_c = S("ones_c", [128, 1], BF16)
    eps_c = S("eps_c", [128, 1], F32)

    x_sb = S("x_sb", [128, 2, STW, D], F32)
    xb = S("xb", [128, 4, D], BF16)
    xfm = S("xfm", [128, NCH, STW * W], BF16)
    qfm = S("qfm", [128, NCH, STW * W], BF16)
    kfm = S("kfm", [128, NCH, STW * W], BF16)
    kprev = S("kprev", [128, NCH, W], BF16)
    v_sb = S("v_sb", [128, STW, D], BF16)
    vprev = S("vprev", [128, D], BF16)
    ep = S("ep", [128, 8, 512], BF16)
    ao = S("ao", [128, D], BF16)
    aofm = S("aofm", [128, NCH, W], BF16)
    r16 = S("r16", [128, 16], F32)
    y1 = S("y1", [128, D], F32)
    y2 = S("y2", [128, D], F32)
    x2 = S("x2", [128, 2, D], F32)
    x2b = S("x2b", [128, 2, D], BF16)
    x2fm = S("x2fm", [128, 2, NCH, W], BF16)
    xr = S("xr", [32, 128], BF16)
    outb = S("outb", [128, D], F32)
    bo_row = x2[0:1, 0, :]
    bf_row = x2[0:1, 1, :]
    cols = S("cols", [128, 16], F32)

    C_SUM0, C_SUM1, C_SUM, C_M, C_M2, C_VAR, C_LNV, C_RSTD, C_NMR = range(9)

    ps = [P(f"ps{i}", [128, 512], F32) for i in range(4)]
    u0 = P("u0", [128, 512], F32)
    u1 = P("u1", [128, 512], F32)
    p0 = P("p0", [128, 512], F32)
    p1 = P("p1", [128, 512], F32)

    sem_pe = ctx.enter_context(nc.semaphore("sem_pe"))
    sem_ac = ctx.enter_context(nc.semaphore("sem_ac"))
    sem_ve = ctx.enter_context(nc.semaphore("sem_ve"))
    sem_gp = ctx.enter_context(nc.semaphore("sem_gp"))
    dma_sems = {k: ctx.enter_context(nc.semaphore(f"sem_{k}"))
                for k in ("di0", "di1", "dx0", "dx1", "dc", "do0", "do1",
                          "dt0", "dt1")}
    sems = {"pe": sem_pe, "ac": sem_ac, "ve": sem_ve, "gp": sem_gp, **dma_sems}
    block = ctx.enter_context(nc.Block())

    R = {name: Res() for name in
         ["xb0", "xb1", "xb2", "xb3", "xfm", "qfm", "kfm", "kprev", "v",
          "vprev", "ao", "aofm", "r16", "y1", "y2",
          "x2_0", "x2_1", "x2b_0", "x2b_1", "x2fm_0", "x2fm_1", "xr",
          "u0", "u1", "p0", "p1", "stgA", "stgB", "stgC",
          "cols1", "cols2", "ident", "m01", "ones"]}
    for i in range(4):
        R[f"s{i}"] = Res()
    for g in range(8):
        R[f"ep{g}"] = Res()
    for par in range(2):
        R[f"x{par}"] = Res()
        R[f"out{par}"] = Res()

    with ctx:
        t = kb.emit("gp", lambda e: e.memset(ident[:], 0.0), inc="gp")
        t = kb.emit("gp", lambda e: e.affine_select(
            out=ident[:], in_=ident[:], compare_op=ALU.not_equal, fill=1.0,
            base=0, pattern=[[-1, 128]], channel_multiplier=1),
            waits=[t], inc="gp")
        R["ident"].did_wr(t)
        t = kb.emit("gp", lambda e: e.memset(m01[:], 0.0), inc="gp")
        t = kb.emit("gp", lambda e: e.affine_select(
            out=m01[:], in_=m01[:], compare_op=ALU.is_gt, fill=1.0,
            base=0, pattern=[[-1, 128]], channel_multiplier=1),
            waits=[t], inc="gp")
        R["m01"].did_wr(t)
        kb.emit("gp", lambda e: e.memset(ones_r[:], 1.0), inc="gp")
        kb.emit("gp", lambda e: e.memset(eps_c[:], EPS), inc="gp")
        t = kb.emit("gp", lambda e: e.memset(ones_c[:], 1.0), inc="gp")
        onesall = ("gp", kb.cnt["gp"])
        R["ones"].did_wr(onesall)

        stg = [(y1, R["stgA"]), (y2, R["stgB"])]

        def load_cast(i, src_ap, dst_ap, eng):
            buf, res = stg[i % 2]
            tok = kb.emit("sy", lambda e, s=src_ap, b=buf: e.dma_start(
                out=b[:, 0:s.shape[1]], in_=s), waits=res.wr(),
                inc=f"di{i % 2}", n=16)
            res.did_wr(tok)
            if eng == "ac":
                fn = (lambda e, b=buf, d=dst_ap, f=src_ap.shape[1]:
                      e.copy(d, b[:, 0:f]))
            else:
                fn = (lambda e, b=buf, d=dst_ap, f=src_ap.shape[1]:
                      e.tensor_copy(out=d, in_=b[:, 0:f]))
            tok = kb.emit(eng, fn, waits=res.rd(), inc=eng)
            res.did_rd(tok)
            return tok

        i = 0
        w_toks = []
        for mat, dst in [(wq_h, wq), (wk_h, wk), (wv_h, wv), (wo_h, wo)]:
            for ci in range(NCH):
                w_toks.append(load_cast(i, mat[ci * 128:(ci + 1) * 128, :],
                                        dst[:, ci, :], "ve" if i % 2 else "ac"))
                i += 1
        for ci in range(NCH):
            w_toks.append(load_cast(i, vr_h[ci * 128:(ci + 1) * 128, :],
                                    vr[:, ci, :], "ve" if i % 2 else "ac"))
            i += 1
        for oc in range(NCH):
            buf, res = stg[i % 2]
            tok = kb.emit("sy", lambda e, s=wf_h[oc * 128:(oc + 1) * 128, :], b=buf:
                          e.dma_start(out=b[:], in_=s), waits=res.wr(),
                          inc=f"di{i % 2}", n=16)
            res.did_wr(tok)
            tok = kb.emit("ve", lambda e, b=buf: e.tensor_copy(out=xb[:, 0, :], in_=b[:]),
                          waits=res.rd() + R["xb0"].wr(), inc="ve")
            res.did_rd(tok)
            R["xb0"].did_wr(tok)
            p0bf = p0[:].bitcast(BF16)
            fns = [
                (lambda e, ci=ci, pb=p0bf: e.transpose(
                    pb[0:128, ci * 128:(ci + 1) * 128],
                    xb[:, 0, ci * 128:(ci + 1) * 128], ident[:]))
                for ci in range(NCH)]
            tok = kb.group("pe", fns,
                           waits=R["xb0"].rd() + R["ident"].rd() + R["p0"].wr(),
                           inc="pe")
            R["xb0"].did_rd(tok)
            R["p0"].did_wr(tok)
            tok = kb.emit("ac", lambda e, oc=oc, pb=p0bf: e.activation(
                wt[:, :, oc * 128:(oc + 1) * 128],
                pb[0:128, 0:1024].rearrange("p (c f) -> p c f", c=NCH), AF.Copy),
                waits=R["p0"].rd(), inc="ac")
            R["p0"].did_rd(tok)
            w_toks.append(tok)
            i += 1
        for oc in range(NCH):
            buf, res = stg[i % 2]
            tok = kb.emit("sy", lambda e, s=cm_h[oc * 128:(oc + 1) * 128, :], b=buf:
                          e.dma_start(out=b[:, 0:32], in_=s), waits=res.wr(),
                          inc=f"di{i % 2}", n=16)
            res.did_wr(tok)
            tok = kb.emit("ve", lambda e, b=buf: e.tensor_copy(
                out=xb[:, 0, 0:32], in_=b[:, 0:32]),
                waits=res.rd() + R["xb0"].wr(), inc="ve")
            res.did_rd(tok)
            R["xb0"].did_wr(tok)
            p1bf = p1[:].bitcast(BF16)
            tok = kb.emit("pe", lambda e, pb=p1bf: e.transpose(
                pb[0:32, 0:128], xb[:, 0, 0:32], ident[:]),
                waits=R["xb0"].rd() + R["ident"].rd() + R["p1"].wr(), inc="pe")
            R["xb0"].did_rd(tok)
            R["p1"].did_wr(tok)
            tok = kb.emit("ac", lambda e, oc=oc, pb=p1bf: e.activation(
                ct[:, oc * 128:(oc + 1) * 128], pb[0:32, 0:128], AF.Copy),
                waits=R["p1"].rd(), inc="ac")
            R["p1"].did_rd(tok)
            w_toks.append(tok)
            i += 1
        for src_h, dst in [(bq_h, bqc), (bk_h, bkc), (bv_h, bvc)]:
            for c in range(NCH):
                tok = kb.emit("sy", lambda e, s=src_h[c * 128:(c + 1) * 128], d=dst, c_=c:
                              e.dma_start(out=d[:, c_:c_ + 1],
                                          in_=s.rearrange("(p o) -> p o", o=1)),
                              inc="dc", n=16)
                w_toks.append(tok)
        tok = kb.emit("sy", lambda e: e.dma_start(out=bo_row, in_=bo_h.rearrange("(o f) -> o f", o=1)), inc="dc", n=16)
        w_toks.append(tok)
        tok = kb.emit("sy", lambda e: e.dma_start(out=bf_row, in_=bf_h.rearrange("(o f) -> o f", o=1)), inc="dc", n=16)
        w_toks.append(tok)
        rows_di = kb.cnt["dc"]
        bvb_tok = kb.emit("ve", lambda e: e.tensor_copy(out=bvb[:], in_=bvc[:]),
                          waits=[("dc", rows_di)], inc="ve")
        for half in range(2):
            bank = p0 if half == 0 else p1
            bres = R["p0"] if half == 0 else R["p1"]
            fns = [
                (lambda e, ci=ci, b=bank, h=half: e.matmul(
                    b[0:1, 0:512], bvb[:, ci:ci + 1],
                    wo[:, ci, h * 512:(h + 1) * 512],
                    start=(ci == 0), stop=(ci == NCH - 1)))
                for ci in range(NCH)]
            tok = kb.group("pe", fns,
                           waits=[bvb_tok] + w_toks[24:32] + bres.wr(), inc="pe")
            bres.did_wr(tok)
            tok = kb.emit("ve", lambda e, b=bank, h=half: e.tensor_tensor(
                out=bo_row[0:1, h * 512:(h + 1) * 512], in0=b[0:1, 0:512],
                in1=bo_row[0:1, h * 512:(h + 1) * 512], op=ALU.add),
                waits=bres.rd() + [("dc", rows_di)], inc="ve")
            bres.did_rd(tok)
            kb.emit("ve", lambda e, h=half: e.tensor_copy(
                out=bcomb[:, h * 512:(h + 1) * 512],
                in_=bo_row[0:1, h * 512:(h + 1) * 512]), inc="ve")
        bffn_tok = kb.emit("ve", lambda e: e.tensor_copy(out=bffn[:], in_=bf_row),
                           waits=[("dc", rows_di)], inc="ve")
        for row_bf, bc in [(bcomb, bcomb_bc), (bffn, bffn_bc)]:
            for half in range(2):
                bank = p0 if half == 0 else p1
                bres = R["p0"] if half == 0 else R["p1"]
                tok = kb.emit("pe", lambda e, b=bank, h=half, rb=row_bf: e.matmul(
                    b[:], ones_r[:], rb[:, h * 512:(h + 1) * 512],
                    start=True, stop=True),
                    waits=[("ve", kb.cnt["ve"])] + bres.wr() + R["ones"].rd(),
                    inc="pe")
                bres.did_wr(tok)
                tok = kb.emit("ve", lambda e, b=bank, h=half, d=bc: e.tensor_copy(
                    out=d[:, h * 512:(h + 1) * 512], in_=b[:]),
                    waits=bres.rd(), inc="ve")
                bres.did_rd(tok)
        setup_done = [("ve", kb.cnt["ve"]), ("ac", kb.cnt["ac"]),
                      ("pe", kb.cnt["pe"])]

        def load_x(st):
            par = st % 2
            res = R[f"x{par}"]
            tok = kb.emit("sy", lambda e, st=st, par=par: e.dma_start(
                out=x_sb[:, par, :, :],
                in_=x_h[st * STW * W:(st + 1) * STW * W, :].rearrange(
                    "(w p) f -> p w f", p=128)),
                waits=res.wr(), inc=f"dx{par}", n=16)
            res.did_wr(tok)

        load_x(0)

        def head_slice(buf, h, fr):
            return buf[64 * (h % 2):64 * (h % 2) + 64, h // 2, fr]

        def post_A1(gi):
            u1bf = u1[:].bitcast(BF16)
            fns = [(lambda e, c=c: e.transpose(
                u1bf[0:128, c * 128:(c + 1) * 128],
                ao[:, c * 128:(c + 1) * 128], ident[:])) for c in range(NCH)]
            tok = kb.group("pe", fns, waits=R["ao"].rd() + R["u1"].wr(), inc="pe")
            R["ao"].did_rd(tok)
            R["u1"].did_wr(tok)
            tok = kb.emit("ve", lambda e: e.tensor_copy(
                out=aofm[:], in_=u1bf[0:128, 0:1024].rearrange(
                    "p (c f) -> p c f", c=NCH)),
                waits=R["u1"].rd() + R["aofm"].wr(), inc="ve")
            R["u1"].did_rd(tok)
            R["aofm"].did_wr(tok)

        def post_A2(gi):
            for half in range(2):
                bank = p0 if half == 0 else p1
                bres = R["p0"] if half == 0 else R["p1"]
                fns = [(lambda e, ci=ci, b=bank, h=half: e.matmul(
                    b[:], aofm[:, ci, :], wo[:, ci, h * 512:(h + 1) * 512],
                    start=(ci == 0), stop=(ci == NCH - 1))) for ci in range(NCH)]
                tok = kb.group("pe", fns,
                               waits=R["aofm"].rd() + bres.wr(),
                               inc="pe")
                R["aofm"].did_rd(tok)
                bres.did_wr(tok)

        def post_X(gi):
            pw = gi % 2
            u0bf = u0[:].bitcast(BF16)
            x2b_r = R[f"x2b_{pw}"]
            x2fm_r = R[f"x2fm_{pw}"]
            fns = [(lambda e, c=c, pw=pw: e.transpose(
                u0bf[0:128, c * 128:(c + 1) * 128],
                x2b[:, pw, c * 128:(c + 1) * 128], ident[:])) for c in range(NCH)]
            tok = kb.group("pe", fns, waits=x2b_r.rd() + R["u0"].wr(), inc="pe")
            x2b_r.did_rd(tok)
            R["u0"].did_wr(tok)
            tok = kb.emit("ve", lambda e, pw=pw: e.tensor_copy(
                out=x2fm[:, pw], in_=u0bf[0:128, 0:1024].rearrange(
                    "p (c f) -> p c f", c=NCH)),
                waits=R["u0"].rd() + x2fm_r.wr(), inc="ve")
            R["u0"].did_rd(tok)
            x2fm_r.did_wr(tok)

        def post_XR(gi):
            pw = gi % 2
            bi = 0
            x2fm_r = R[f"x2fm_{pw}"]
            s0res = R[f"s{bi}"]
            fns = [(lambda e, ci=ci, pw=pw, bi=bi: e.matmul(
                ps[bi][0:32, 128:256], vr[:, ci, :], x2fm[:, pw, ci, :],
                start=(ci == 0), stop=(ci == NCH - 1))) for ci in range(NCH)]
            tok = kb.group("pe", fns, waits=x2fm_r.rd() + s0res.wr(), inc="pe")
            x2fm_r.did_rd(tok)
            s0res.did_wr(tok)
            tok = kb.emit("ve", lambda e, bi=bi: e.tensor_copy(
                out=xr[:], in_=ps[bi][0:32, 128:256]),
                waits=s0res.rd() + R["xr"].wr(), inc="ve")
            s0res.did_rd(tok)
            R["xr"].did_wr(tok)

        def pre_B(gi):
            wt_ = gi % STW
            par = (gi // STW) % 2
            tok = kb.emit("gp", lambda e, w_=wt_, p_=par: e.tensor_tensor(
                out=y1[:], in0=x_sb[:, p_, w_, :], in1=bcomb_bc[:], op=ALU.add),
                waits=R["y1"].wr() + R[f"x{par}"].rd() + setup_done, inc="gp")
            R[f"x{par}"].did_rd(tok)
            R["y1"].did_wr(tok)

        def post_B1(gi):
            y1pre = R["y1"].rd()
            for half in range(2):
                bank = p0 if half == 0 else p1
                bres = R["p0"] if half == 0 else R["p1"]
                csl = C_SUM0 + half
                tok = kb.emit("ve", lambda e, b=bank, h=half, cs=csl:
                              e.scalar_tensor_tensor(
                                  out=y1[:, h * 512:(h + 1) * 512], in0=b[:],
                                  scalar=1.0,
                                  in1=y1[:, h * 512:(h + 1) * 512],
                                  op0=ALU.mult, op1=ALU.add,
                                  accum_out=cols[:, cs:cs + 1]),
                              waits=bres.rd() + y1pre + R["cols1"].wr(), inc="ve")
                bres.did_rd(tok)
                if half == 1:
                    R["y1"].did_wr(tok)
                    R["cols1"].did_wr(tok)

        def post_B2(gi):
            pw = gi % 2
            _ln(R["y1"], y1, x2[:, pw, :], x2b[:, pw, :], True,
                bf_res=R[f"x2b_{pw}"], f32_res=R[f"x2_{pw}"])

        def post_CORR(gi):
            for half in range(2):
                bank = p0 if half == 0 else p1
                bres = R["p0"] if half == 0 else R["p1"]
                tok = kb.emit("pe", lambda e, b=bank, h=half: e.matmul(
                    b[:], xr[:], ct[:, h * 512:(h + 1) * 512],
                    start=True, stop=False),
                    waits=R["xr"].rd() + bres.wr(), inc="pe")
                R["xr"].did_rd(tok)
                bres.did_wr(tok)

        def post_C1(gi):
            pw = gi % 2
            x2fm_r = R[f"x2fm_{pw}"]
            for half in range(2):
                bank = p0 if half == 0 else p1
                bres = R["p0"] if half == 0 else R["p1"]
                fns = [(lambda e, ci=ci, b=bank, h=half, pw=pw: e.matmul(
                    b[:], x2fm[:, pw, ci, :], wt[:, ci, h * 512:(h + 1) * 512],
                    start=False, stop=(ci == NCH - 1))) for ci in range(NCH)]
                tok = kb.group("pe", fns,
                               waits=x2fm_r.rd() + bres.wr(),
                               inc="pe")
                x2fm_r.did_rd(tok)
                bres.did_wr(tok)

        def pre_C2(gi):
            pw = gi % 2
            tok = kb.emit("gp", lambda e, pw=pw: e.tensor_tensor(
                out=y2[:], in0=x2[:, pw, :], in1=bffn_bc[:], op=ALU.add),
                waits=R["y2"].wr() + R[f"x2_{pw}"].rd() + setup_done, inc="gp")
            R[f"x2_{pw}"].did_rd(tok)
            R["y2"].did_wr(tok)

        def post_C2(gi):
            opar = gi % 2
            pw = gi % 2
            y2pre = R["y2"].rd()
            for half in range(2):
                bank = p0 if half == 0 else p1
                bres = R["p0"] if half == 0 else R["p1"]
                csl = C_SUM0 + half
                tok = kb.emit("ve", lambda e, b=bank, h=half, cs=csl:
                              e.scalar_tensor_tensor(
                                  out=y2[:, h * 512:(h + 1) * 512], in0=b[:],
                                  scalar=1.0, in1=y2[:, h * 512:(h + 1) * 512],
                                  op0=ALU.mult, op1=ALU.add,
                                  accum_out=cols[:, cs:cs + 1]),
                              waits=bres.rd() + y2pre + R["cols1"].wr(), inc="ve")
                bres.did_rd(tok)
                if half == 1:
                    R["y2"].did_wr(tok)
                    R["cols1"].did_wr(tok)
            ores = R["out0"]
            _ln(R["y2"], y2, outb[:], None, False, out_res=ores)
            tok = kb.emit("gp", lambda e, gi=gi, op=opar: e.dma_start(
                out=out_h[gi * W:(gi + 1) * W, :], in_=outb[:]),
                waits=ores.rd(), inc=f"do{opar}", n=16)
            ores.did_rd(tok)

        def _ln(yres, ybuf, out_f32, out_bf, also_bf, out_res=None,
                bf_res=None, f32_res=None):
            if not isinstance(out_f32, bass.AP):
                out_f32 = out_f32[:]
            if out_bf is not None and not isinstance(out_bf, bass.AP):
                out_bf = out_bf[:]
            tok = kb.emit("ve", lambda e: e.tensor_tensor(
                out=cols[:, C_SUM:C_SUM + 1], in0=cols[:, C_SUM0:C_SUM0 + 1],
                in1=cols[:, C_SUM1:C_SUM1 + 1], op=ALU.add),
                waits=R["cols1"].rd(), inc="ve")
            kb.emit("ve", lambda e: e.tensor_scalar(
                out=cols[:, C_M:C_M + 1], in0=cols[:, C_SUM:C_SUM + 1],
                scalar1=1.0 / D, scalar2=None, op0=ALU.mult), inc="ve")
            tok = kb.emit("ac", lambda e: e.activation(
                aofm[:].rearrange("p c f -> p (c f)"), ybuf[:], AF.Square,
                accum_out=cols[:, C_VAR:C_VAR + 1]),
                waits=yres.rd() + R["aofm"].wr() + R["cols2"].wr(), inc="ac")
            yres.did_rd(tok)
            R["aofm"].did_wr(tok)
            sq_tok = tok
            kb.emit("ve", lambda e: e.tensor_tensor(
                out=cols[:, C_M2:C_M2 + 1], in0=cols[:, C_M:C_M + 1],
                in1=cols[:, C_M:C_M + 1], op=ALU.mult), inc="ve")
            tok = kb.emit("ve", lambda e: e.scalar_tensor_tensor(
                out=cols[:, C_VAR:C_VAR + 1], in0=cols[:, C_VAR:C_VAR + 1],
                scalar=1.0 / D, in1=cols[:, C_M2:C_M2 + 1],
                op0=ALU.mult, op1=ALU.subtract),
                waits=[sq_tok], inc="ve")
            tok = kb.emit("ac", lambda e: e.activation(
                cols[:, C_LNV:C_LNV + 1], cols[:, C_VAR:C_VAR + 1], AF.Ln,
                bias=eps_c[:], scale=1.0),
                waits=[tok] + R["ones"].rd(), inc="ac")
            rstd_tok = kb.emit("ac", lambda e: e.activation(
                cols[:, C_RSTD:C_RSTD + 1], cols[:, C_LNV:C_LNV + 1], AF.Exp,
                scale=-0.5), inc="ac")
            nmr_tok = kb.emit("ve", lambda e: e.tensor_scalar(
                out=cols[:, C_NMR:C_NMR + 1], in0=cols[:, C_M:C_M + 1],
                scalar1=cols[:, C_RSTD:C_RSTD + 1], scalar2=-1.0,
                op0=ALU.mult, op1=ALU.mult), waits=[rstd_tok], inc="ve")
            R["cols2"].did_wr(nmr_tok)
            if also_bf:
                tok = kb.emit("ac", lambda e, o=out_bf: e.activation(
                    o, ybuf[:], AF.Identity, bias=cols[:, C_NMR:C_NMR + 1],
                    scale=cols[:, C_RSTD:C_RSTD + 1]),
                    waits=[nmr_tok] + yres.rd() + bf_res.wr(), inc="ac")
                yres.did_rd(tok)
                bf_res.did_wr(tok)
            w_extra = out_res.wr() if out_res is not None else []
            if f32_res is not None:
                w_extra = w_extra + f32_res.wr()
            tok = kb.emit("ac", lambda e, o=out_f32: e.activation(
                o, ybuf[:], AF.Identity, bias=cols[:, C_NMR:C_NMR + 1],
                scale=cols[:, C_RSTD:C_RSTD + 1]),
                waits=[nmr_tok] + yres.rd() + w_extra, inc="ac")
            yres.did_rd(tok)
            if out_res is not None:
                out_res.did_wr(tok)
            if f32_res is not None:
                f32_res.did_wr(tok)
            if also_bf:
                R["cols2"].did_rd(tok)

        def cast_xb(stc):
            parc = stc % 2
            for w2 in range(STW):
                xres = R[f"xb{w2}"]
                tok = kb.emit("gp", lambda e, w2=w2, p_=parc:
                              e.tensor_copy(out=xb[:, w2, :],
                                            in_=x_sb[:, p_, w2, :]),
                              waits=R[f"x{parc}"].rd() + xres.wr(), inc="gp")
                R[f"x{parc}"].did_rd(tok)
                xres.did_wr(tok)
                xw = R["xfm"].wr() if w2 == 0 else []
                for c in range(NCH):
                    tok = kb.emit("sy", lambda e, w2=w2, c=c: e.dma_start_transpose(
                        xfm[:, c, w2 * W:(w2 + 1) * W],
                        xb[:, w2, c * 128:(c + 1) * 128]),
                        waits=xres.rd() + xw, inc="dt0", n=16)
                    xw = []
            batch_end = ("dt0", kb.cnt["dt0"])
            for w2 in range(STW):
                R[f"xb{w2}"].did_rd(batch_end)
            R["xfm"].did_wr(batch_end)

        for gi in (range(NW + 2) if PHASE >= 1 else []):
            att = gi < NW
            st, wt_ = gi // STW, gi % STW
            par = st % 2
            first = (gi % WPS == 0)

            if gi - 1 >= 0 and gi - 1 < NW and PHASE >= 4:
                post_A1(gi - 1)
                pre_B(gi - 1)
            if gi - 2 >= 0 and PHASE >= 4:
                post_X(gi - 2)
                pre_C2(gi - 2)

            if att and wt_ == 0:
                if st == 0:
                    cast_xb(0)
                if st > 0:
                    tok = kb.emit("gp", lambda e: e.tensor_copy(
                        out=kprev[:], in_=kfm[:, :, 3 * W:4 * W]),
                        waits=R["kfm"].wr() + R["kprev"].wr(), inc="gp")
                    R["kprev"].did_wr(tok)
                    R["kfm"].did_rd(tok)
                    tok = kb.emit("gp", lambda e: e.tensor_copy(
                        out=vprev[:], in_=v_sb[:, 3, :]),
                        waits=R["v"].wr() + R["vprev"].wr(), inc="gp")
                    R["vprev"].did_wr(tok)
                    R["v"].did_rd(tok)

                qkv_banks = [(p0, R["p0"]), (p1, R["p1"]),
                             (ps[2], R["s2"]), (ps[3], R["s3"])]
                for dst, wmat, bias, dres in [(qfm, wq, bqc, R["qfm"]),
                                              (kfm, wk, bkc, R["kfm"])]:
                    for co in range(NCH):
                        bank, bres = qkv_banks[co % 4]
                        fns = [(lambda e, ci=ci, b=bank, co=co, wm=wmat: e.matmul(
                            b[:], wm[:, ci, co * 128:(co + 1) * 128],
                            xfm[:, ci, :], start=(ci == 0), stop=(ci == NCH - 1)))
                            for ci in range(NCH)]
                        tok = kb.group("pe", fns,
                                       waits=R["xfm"].rd() + bres.wr()
                                       + (setup_done if gi == 0 else []), inc="pe")
                        R["xfm"].did_rd(tok)
                        bres.did_wr(tok)
                        tok = kb.emit("ve", lambda e, b=bank, co=co, d=dst, bi=bias:
                                      e.tensor_scalar(
                                          out=d[:, co, :], in0=b[:],
                                          scalar1=bi[:, co:co + 1], scalar2=None,
                                          op0=ALU.add),
                                      waits=bres.rd()
                                      + (dres.wr() if co == 0 else []), inc="ve")
                        bres.did_rd(tok)
                        if co == NCH - 1:
                            dres.did_wr(tok)
                for half in range(2):
                    for tc in range(STW):
                        bank, bres = qkv_banks[(half * STW + tc) % 4]
                        fns = [(lambda e, ci=ci, b=bank, tc=tc, h=half: e.matmul(
                            b[:], xfm[:, ci, tc * W:(tc + 1) * W],
                            wv[:, ci, h * 512:(h + 1) * 512],
                            start=(ci == 0), stop=(ci == NCH - 1)))
                            for ci in range(NCH)]
                        tok = kb.group("pe", fns, waits=R["xfm"].rd() + bres.wr(),
                                       inc="pe")
                        R["xfm"].did_rd(tok)
                        bres.did_wr(tok)
                        tok = kb.emit("ac", lambda e, b=bank, tc=tc, h=half:
                                      e.copy(v_sb[:, tc, h * 512:(h + 1) * 512],
                                             b[:]),
                                      waits=bres.rd()
                                      + (R["v"].wr() if (half == 0 and tc == 0)
                                         else []), inc="ac")
                        bres.did_rd(tok)
                        if half == 1 and tc == STW - 1:
                            R["v"].did_wr(tok)

            if att and wt_ == 1 and st + 1 < NST:
                load_x(st + 1)
                cast_xb(st + 1)

            for g in (range(8) if att and PHASE >= 2 else []):
                sb_banks = [(ps[0], "s0"), (ps[1], "s1"), (ps[2], "s2"),
                            (ps[3], "s3"), (u0, "u0"), (u1, "u1")]
                bank, bname = sb_banks[g % 6]
                bres = R[bname]
                fns = []
                for hh in range(2):
                    h = g + 8 * hh
                    q_sl = head_slice(qfm, h, slice(wt_ * W, (wt_ + 1) * W))
                    jts = [] if first else [0]
                    jts.append(1)
                    for jt in jts:
                        if jt == 0:
                            if wt_ == 0:
                                k_sl = head_slice(kprev, h, slice(0, W))
                            else:
                                k_sl = head_slice(kfm, h, slice((wt_ - 1) * W, wt_ * W))
                        else:
                            k_sl = head_slice(kfm, h, slice(wt_ * W, (wt_ + 1) * W))
                        reg = (2 * hh + jt) * 128
                        fns.append(lambda e, b=bank, r=reg, ks=k_sl, qs=q_sl:
                                   e.matmul(b[:, r:r + 128], ks, qs,
                                            start=True, stop=True))
                w_ = (bres.wr() + R["qfm"].rd()
                      + (R["kprev"].rd() if (wt_ == 0 and not first) else [])
                      + R["kfm"].rd())
                tok = kb.group("pe", fns, waits=w_, inc="pe")
                bres.did_wr(tok)
                R["qfm"].did_rd(tok)
                R["kfm"].did_rd(tok)
                if wt_ == 0 and not first:
                    R["kprev"].did_rd(tok)
                eres = R[f"ep{g}"]
                if SUB < 1:
                    continue
                if first:
                    tok = kb.emit("ac", lambda e, b=bank, g=g: e.activation(
                        ep[:, g, :].rearrange("p (hh x) -> p hh x", hh=2)[:, :, 128:256],
                        b[:].rearrange("p (hh x) -> p hh x", hh=2)[:, :, 128:256],
                        AF.Exp, scale=0.125),
                        waits=bres.rd() + eres.wr(), inc="ac")
                else:
                    tok = kb.emit("ac", lambda e, b=bank, g=g: e.activation(
                        ep[:, g, :], b[:], AF.Exp, scale=0.125),
                        waits=bres.rd() + eres.wr(), inc="ac")
                bres.did_rd(tok)
                eres.did_wr(tok)

            if gi - 2 >= 0 and PHASE >= 4:
                post_XR(gi - 2)
            if gi - 1 >= 0 and gi - 1 < NW and PHASE >= 4:
                post_A2(gi - 1)

            if att and PHASE >= 3 and SUB >= 2:
                ep_cur = ep[:].rearrange("p g (hhjt x) -> p (g hhjt) x",
                                         x=128)[:, 1::2, :]
                mask_waits = []
                for g in range(8):
                    mask_waits += R[f"ep{g}"].rd()
                tok = kb.emit("ve", lambda e, epc=ep_cur: e.tensor_tensor(
                    out=epc, in0=epc,
                    in1=m01[:].unsqueeze(1).broadcast_to([128, 16, 128]),
                    op=ALU.mult),
                    waits=mask_waits + R["m01"].rd(), inc="ve")
                for g in range(8):
                    R[f"ep{g}"].did_wr(tok)

            if att and PHASE >= 3:
                fns = []
                sum_waits = R["u0"].wr() + R["ones"].rd()
                for h in range(H):
                    g, hh = h % 8, h // 8
                    jts = [1] if first else [0, 1]
                    for jt in jts:
                        reg = (2 * hh + jt) * 128
                        st_ = (len(jts) == 1) or (jt == jts[0])
                        sp_ = (len(jts) == 1) or (jt == jts[-1])
                        fns.append(lambda e, g=g, r=reg, h=h, st=st_, sp=sp_:
                                   e.matmul(u0[:, h:h + 1], ep[:, g, r:r + 128],
                                            ones_c[:], start=st, stop=sp))
                for g in range(8):
                    sum_waits += R[f"ep{g}"].rd()
                tok = kb.group("pe", fns, waits=sum_waits, inc="pe")
                R["u0"].did_wr(tok)
                for g in range(8):
                    R[f"ep{g}"].did_rd(tok)
                tok = kb.emit("ve", lambda e: e.reciprocal(out=r16[:],
                                                           in_=u0[:, 0:16]),
                              waits=R["u0"].rd() + R["r16"].wr(), inc="ve")
                R["u0"].did_rd(tok)
                R["r16"].did_wr(tok)

            if gi - 1 >= 0 and gi - 1 < NW and PHASE >= 4:
                post_B1(gi - 1)
                post_B2(gi - 1)
            if gi - 2 >= 0 and PHASE >= 4:
                post_CORR(gi - 2)

            if att and PHASE >= 3:
                for bank_i in range(2):
                    bank = u0 if bank_i == 0 else u1
                    bres = R["u0"] if bank_i == 0 else R["u1"]
                    fns = []
                    uw = bres.wr() + R["v"].rd() + (
                        R["vprev"].rd() if (wt_ == 0 and not first) else [])
                    for h8 in range(8):
                        h = bank_i * 8 + h8
                        g, hh = h % 8, h // 8
                        jts = [1] if first else [0, 1]
                        for jt in jts:
                            reg = (2 * hh + jt) * 128
                            if jt == 0:
                                if wt_ == 0:
                                    v_sl = vprev[:, h * 64:(h + 1) * 64]
                                else:
                                    v_sl = v_sb[:, wt_ - 1, h * 64:(h + 1) * 64]
                            else:
                                v_sl = v_sb[:, wt_, h * 64:(h + 1) * 64]
                            st_ = (len(jts) == 1) or (jt == jts[0])
                            sp_ = (len(jts) == 1) or (jt == jts[-1])
                            fns.append(lambda e, b=bank, g=g, r=reg, vs=v_sl,
                                       h8=h8, st=st_, sp=sp_:
                                       e.matmul(b[:, h8 * 64:(h8 + 1) * 64],
                                                ep[:, g, r:r + 128], vs,
                                                start=st, stop=sp))
                    tok = kb.group("pe", fns, waits=uw, inc="pe")
                    bres.did_wr(tok)
                    R["v"].did_rd(tok)
                    if wt_ == 0 and not first:
                        R["vprev"].did_rd(tok)

            if gi - 2 >= 0 and PHASE >= 4:
                post_C1(gi - 2)

            if att and PHASE >= 3:
                for bank_i in range(2):
                    bank = u0 if bank_i == 0 else u1
                    bres = R["u0"] if bank_i == 0 else R["u1"]
                    tok = kb.emit("ve", lambda e, b=bank, bi=bank_i: e.tensor_tensor(
                        out=ao[:, bi * 512:(bi + 1) * 512].rearrange(
                            "p (h f) -> p h f", h=8),
                        in0=b[:].rearrange("p (h f) -> p h f", h=8),
                        in1=r16[:, bi * 8:(bi + 1) * 8].unsqueeze(2).broadcast_to(
                            [128, 8, 64]),
                        op=ALU.mult),
                        waits=bres.rd() + R["r16"].rd()
                        + (R["ao"].wr() if bank_i == 0 else []), inc="ve")
                    bres.did_rd(tok)
                    R["r16"].did_rd(tok)
                    if bank_i == 1:
                        R["ao"].did_wr(tok)

            if gi - 2 >= 0 and PHASE >= 4:
                post_C2(gi - 2)

        kb.emit("gp", lambda e: e.nop(),
                waits=[("do0", kb.cnt["do0"]), ("do1", kb.cnt["do1"])])

        def run(engine_obj, eng_name):
            for fn, ws, inc_sem, inc_n in kb.ops[eng_name]:
                for sname, val in ws:
                    engine_obj.wait_ge(sems[sname], val)
                inst = fn(engine_obj)
                if inc_sem:
                    inst.then_inc(sems[inc_sem], inc_n)

        @block.sync
        def _(e):
            run(e, "sy")

        @block.tensor
        def _(e):
            run(e, "pe")

        @block.scalar
        def _(e):
            run(e, "ac")

        @block.vector
        def _(e):
            run(e, "ve")

        @block.gpsimd
        def _(e):
            run(e, "gp")

    return nc


def kernel(L0_attn_x, L0_attn_Wq, L0_attn_bq, L0_attn_Wk, L0_attn_bk,
           L0_attn_Wv, L0_attn_bv, L0_attn_Wo, L0_attn_bo,
           L0_ffn_W, L0_ffn_b, L0_ffn_V_r, L0_ffn_C):
    x = np.ascontiguousarray(np.asarray(L0_attn_x, dtype=np.float32))
    B, SL, DM = x.shape
    n_cores = 8
    bpc = B // n_cores
    nw = bpc * SL // W

    nc = build_program(nw)

    common = {
        "Wq": np.ascontiguousarray(L0_attn_Wq, np.float32),
        "Wk": np.ascontiguousarray(L0_attn_Wk, np.float32),
        "Wv": np.ascontiguousarray(L0_attn_Wv, np.float32),
        "Wo": np.ascontiguousarray(L0_attn_Wo, np.float32),
        "Wf": np.ascontiguousarray(L0_ffn_W, np.float32),
        "bq": np.ascontiguousarray(L0_attn_bq, np.float32),
        "bk": np.ascontiguousarray(L0_attn_bk, np.float32),
        "bv": np.ascontiguousarray(L0_attn_bv, np.float32),
        "bo": np.ascontiguousarray(L0_attn_bo, np.float32),
        "bf": np.ascontiguousarray(L0_ffn_b, np.float32),
        "Vr": np.ascontiguousarray(L0_ffn_V_r, np.float32),
        "Cm": np.ascontiguousarray(L0_ffn_C, np.float32),
    }
    in_maps = []
    for c in range(n_cores):
        shard = np.ascontiguousarray(
            x[c * bpc:(c + 1) * bpc].reshape(bpc * SL, DM))
        in_maps.append(dict(common, x=shard))

    res = run_bass_kernel_spmd(nc, in_maps, core_ids=list(range(n_cores)))
    out = np.concatenate(
        [r["out"].reshape(bpc, SL, DM) for r in res.results], axis=0)
    return out.astype(np.float32)


# version 23 (final)
# speedup vs baseline: 1.0862x; 1.0862x over previous
import os
import numpy as np
from contextlib import ExitStack

import concourse.bass as bass
import concourse.mybir as mybir
from concourse.bass_utils import run_bass_kernel_spmd

F32 = mybir.dt.float32
BF16 = mybir.dt.bfloat16
AF = mybir.ActivationFunctionType
ALU = mybir.AluOpType

ENGS = ("sy", "pe", "ac", "ve", "gp")
OWN = {"pe": "pe", "ac": None, "ve": None, "gp": None, "sy": None}
CHAINED = ("ve", "ac", "gp")

D = 1024
NCH = 8
H = 16
HD = 64
W = 128
WPS = 32
STW = 4
EPS = 1e-5
DEBUG_BARRIER = False
PHASE = int(os.environ.get("ATH_PHASE", "9"))
SUB = int(os.environ.get("ATH_SUB", "9"))


class KB:

    def __init__(self):
        self.ops = {e: [] for e in ENGS}
        self.cnt = {"pe": 0, "ac": 0, "ve": 0, "gp": 0,
                    "di0": 0, "di1": 0, "dx0": 0, "dx1": 0, "dc": 0,
                    "do0": 0, "do1": 0, "dt0": 0, "dt1": 0}
        self.hwm = {}

    def emit(self, eng, fn, waits=(), inc=None, n=1):
        ws = []
        seen = set()
        wl = list(waits)
        if eng in CHAINED:
            if inc is None:
                inc, n = eng, 1
            prev = self.cnt[eng]
            if prev > 0:
                wl = [(eng, prev)] + wl
        if DEBUG_BARRIER:
            wl = wl + [(s, v) for s, v in self.cnt.items() if v > 0]
        for t in wl:
            if t is None:
                continue
            s, v = t
            if v <= 0 or s == OWN[eng] or s in seen:
                if s in seen:
                    ws = [(s0, max(v0, v) if s0 == s else v0) for s0, v0 in ws]
                continue
            if self.hwm.get((eng, s), 0) >= v:
                continue
            self.hwm[(eng, s)] = v
            seen.add(s)
            ws.append((s, v))
        self.ops[eng].append((fn, ws, inc, n))
        if inc:
            self.cnt[inc] += n
            return (inc, self.cnt[inc])
        return None

    def group(self, eng, fns, waits=(), inc=None, n=1):
        tok = None
        for i, fn in enumerate(fns):
            tok = self.emit(eng, fn, waits if i == 0 else (),
                            inc if i == len(fns) - 1 else None, n)
        return tok


class Res:

    def __init__(self):
        self.wtok = None
        self.rtoks = []

    def rd(self):
        return [self.wtok] if self.wtok else []

    def wr(self):
        out = list(self.rtoks)
        if self.wtok:
            out.append(self.wtok)
        return out

    def did_rd(self, tok):
        assert tok is not None
        self.rtoks.append(tok)

    def did_wr(self, tok):
        assert tok is not None
        self.wtok = tok
        self.rtoks = []


def build_program(n_windows=64):
    assert n_windows % STW == 0
    NW = n_windows
    NST = NW // STW
    TOK = NW * W

    nc = bass.Bass("TRN2", target_bir_lowering=False)

    x_h = nc.dram_tensor("x", [TOK, D], F32, kind="ExternalInput").ap()
    wq_h = nc.dram_tensor("Wq", [D, D], F32, kind="ExternalInput").ap()
    wk_h = nc.dram_tensor("Wk", [D, D], F32, kind="ExternalInput").ap()
    wv_h = nc.dram_tensor("Wv", [D, D], F32, kind="ExternalInput").ap()
    wo_h = nc.dram_tensor("Wo", [D, D], F32, kind="ExternalInput").ap()
    wf_h = nc.dram_tensor("Wf", [D, D], F32, kind="ExternalInput").ap()
    bq_h = nc.dram_tensor("bq", [D], F32, kind="ExternalInput").ap()
    bk_h = nc.dram_tensor("bk", [D], F32, kind="ExternalInput").ap()
    bv_h = nc.dram_tensor("bv", [D], F32, kind="ExternalInput").ap()
    bo_h = nc.dram_tensor("bo", [D], F32, kind="ExternalInput").ap()
    bf_h = nc.dram_tensor("bf", [D], F32, kind="ExternalInput").ap()
    vr_h = nc.dram_tensor("Vr", [D, 32], F32, kind="ExternalInput").ap()
    cm_h = nc.dram_tensor("Cm", [D, 32], F32, kind="ExternalInput").ap()
    out_h = nc.dram_tensor("out", [TOK, D], F32, kind="ExternalOutput").ap()

    kb = KB()
    ctx = ExitStack()
    S = lambda *a: ctx.enter_context(nc.sbuf_tensor(*a))
    P = lambda *a: ctx.enter_context(nc.psum_tensor(*a))

    wq = S("wq", [128, NCH, D], BF16)
    wk = S("wk", [128, NCH, D], BF16)
    wv = S("wv", [128, NCH, D], BF16)
    wo = S("wo", [128, NCH, D], BF16)
    wt = S("wt", [128, NCH, D], BF16)
    vr = S("vr", [128, NCH, 32], BF16)
    ct = S("ct", [32, D], BF16)
    bqc = S("bqc", [128, NCH], F32)
    bkc = S("bkc", [128, NCH], F32)
    bvc = S("bvc", [128, NCH], F32)
    bvb = S("bvb", [128, NCH], BF16)
    bcomb = S("bcomb", [1, D], BF16)
    bffn = S("bffn", [1, D], BF16)
    bcomb_bc = S("bcomb_bc", [128, D], BF16)
    bffn_bc = S("bffn_bc", [128, D], BF16)
    ident = S("ident", [128, 128], BF16)
    m01 = S("m01", [128, 128], BF16)
    ones_r = S("ones_r", [1, 128], BF16)
    ones_c = S("ones_c", [128, 1], BF16)
    eps_c = S("eps_c", [128, 1], F32)

    x_sb = S("x_sb", [128, 2, STW, D], F32)
    xb = S("xb", [128, 4, D], BF16)
    xfm = S("xfm", [128, NCH, STW * W], BF16)
    qfm = S("qfm", [128, NCH, STW * W], BF16)
    kfm = S("kfm", [128, NCH, STW * W], BF16)
    kprev = S("kprev", [128, NCH, W], BF16)
    v_sb = S("v_sb", [128, STW, D], BF16)
    vprev = S("vprev", [128, D], BF16)
    ep = S("ep", [128, 8, 512], BF16)
    ao = S("ao", [128, D], BF16)
    aofm = S("aofm", [128, NCH, W], BF16)
    r16 = S("r16", [128, 16], F32)
    y1 = S("y1", [128, D], F32)
    y2 = S("y2", [128, D], F32)
    x2 = S("x2", [128, 2, D], F32)
    x2b = S("x2b", [128, 2, D], BF16)
    x2fm = S("x2fm", [128, 2, NCH, W], BF16)
    xr = S("xr", [32, 128], BF16)
    outb = S("outb", [128, D], F32)
    bo_row = x2[0:1, 0, :]
    bf_row = x2[0:1, 1, :]
    cols = S("cols", [128, 16], F32)

    C_SUM0, C_SUM1, C_SUM, C_M, C_M2, C_VAR, C_LNV, C_RSTD, C_NMR = range(9)

    ps = [P(f"ps{i}", [128, 512], F32) for i in range(4)]
    u0 = P("u0", [128, 512], F32)
    u1 = P("u1", [128, 512], F32)
    p0 = P("p0", [128, 512], F32)
    p1 = P("p1", [128, 512], F32)

    sem_pe = ctx.enter_context(nc.semaphore("sem_pe"))
    sem_ac = ctx.enter_context(nc.semaphore("sem_ac"))
    sem_ve = ctx.enter_context(nc.semaphore("sem_ve"))
    sem_gp = ctx.enter_context(nc.semaphore("sem_gp"))
    dma_sems = {k: ctx.enter_context(nc.semaphore(f"sem_{k}"))
                for k in ("di0", "di1", "dx0", "dx1", "dc", "do0", "do1",
                          "dt0", "dt1")}
    sems = {"pe": sem_pe, "ac": sem_ac, "ve": sem_ve, "gp": sem_gp, **dma_sems}
    block = ctx.enter_context(nc.Block())

    R = {name: Res() for name in
         ["xb0", "xb1", "xb2", "xb3", "xfm", "qfm", "kfm", "kprev", "v",
          "vprev", "ao", "aofm", "r16", "y1", "y2",
          "x2_0", "x2_1", "x2b_0", "x2b_1", "x2fm_0", "x2fm_1", "xr",
          "u0", "u1", "p0", "p1", "stgA", "stgB", "stgC",
          "cols1", "cols2", "ident", "m01", "ones"]}
    for i in range(4):
        R[f"s{i}"] = Res()
    for g in range(8):
        R[f"ep{g}"] = Res()
    for par in range(2):
        R[f"x{par}"] = Res()
        R[f"out{par}"] = Res()

    with ctx:
        t = kb.emit("gp", lambda e: e.memset(ident[:], 0.0), inc="gp")
        t = kb.emit("gp", lambda e: e.affine_select(
            out=ident[:], in_=ident[:], compare_op=ALU.not_equal, fill=1.0,
            base=0, pattern=[[-1, 128]], channel_multiplier=1),
            waits=[t], inc="gp")
        R["ident"].did_wr(t)
        t = kb.emit("gp", lambda e: e.memset(m01[:], 0.0), inc="gp")
        t = kb.emit("gp", lambda e: e.affine_select(
            out=m01[:], in_=m01[:], compare_op=ALU.is_gt, fill=1.0,
            base=0, pattern=[[-1, 128]], channel_multiplier=1),
            waits=[t], inc="gp")
        R["m01"].did_wr(t)
        kb.emit("gp", lambda e: e.memset(ones_r[:], 1.0), inc="gp")
        kb.emit("gp", lambda e: e.memset(eps_c[:], EPS), inc="gp")
        t = kb.emit("gp", lambda e: e.memset(ones_c[:], 1.0), inc="gp")
        onesall = ("gp", kb.cnt["gp"])
        R["ones"].did_wr(onesall)

        stg = [(y1, R["stgA"]), (y2, R["stgB"])]

        def load_cast(i, src_ap, dst_ap, eng):
            buf, res = stg[i % 2]
            tok = kb.emit("sy", lambda e, s=src_ap, b=buf: e.dma_start(
                out=b[:, 0:s.shape[1]], in_=s), waits=res.wr(),
                inc=f"di{i % 2}", n=16)
            res.did_wr(tok)
            if eng == "ac":
                fn = (lambda e, b=buf, d=dst_ap, f=src_ap.shape[1]:
                      e.copy(d, b[:, 0:f]))
            else:
                fn = (lambda e, b=buf, d=dst_ap, f=src_ap.shape[1]:
                      e.tensor_copy(out=d, in_=b[:, 0:f]))
            tok = kb.emit(eng, fn, waits=res.rd(), inc=eng)
            res.did_rd(tok)
            return tok

        i = 0
        w_toks = []
        for mat, dst in [(wq_h, wq), (wk_h, wk), (wv_h, wv), (wo_h, wo)]:
            for ci in range(NCH):
                w_toks.append(load_cast(i, mat[ci * 128:(ci + 1) * 128, :],
                                        dst[:, ci, :], "ve" if i % 2 else "ac"))
                i += 1
        for ci in range(NCH):
            w_toks.append(load_cast(i, vr_h[ci * 128:(ci + 1) * 128, :],
                                    vr[:, ci, :], "ve" if i % 2 else "ac"))
            i += 1
        for oc in range(NCH):
            buf, res = stg[i % 2]
            tok = kb.emit("sy", lambda e, s=wf_h[oc * 128:(oc + 1) * 128, :], b=buf:
                          e.dma_start(out=b[:], in_=s), waits=res.wr(),
                          inc=f"di{i % 2}", n=16)
            res.did_wr(tok)
            tok = kb.emit("ve", lambda e, b=buf: e.tensor_copy(out=xb[:, 0, :], in_=b[:]),
                          waits=res.rd() + R["xb0"].wr(), inc="ve")
            res.did_rd(tok)
            R["xb0"].did_wr(tok)
            p0bf = p0[:].bitcast(BF16)
            fns = [
                (lambda e, ci=ci, pb=p0bf: e.transpose(
                    pb[0:128, ci * 128:(ci + 1) * 128],
                    xb[:, 0, ci * 128:(ci + 1) * 128], ident[:]))
                for ci in range(NCH)]
            tok = kb.group("pe", fns,
                           waits=R["xb0"].rd() + R["ident"].rd() + R["p0"].wr(),
                           inc="pe")
            R["xb0"].did_rd(tok)
            R["p0"].did_wr(tok)
            tok = kb.emit("ac", lambda e, oc=oc, pb=p0bf: e.activation(
                wt[:, :, oc * 128:(oc + 1) * 128],
                pb[0:128, 0:1024].rearrange("p (c f) -> p c f", c=NCH), AF.Copy),
                waits=R["p0"].rd(), inc="ac")
            R["p0"].did_rd(tok)
            w_toks.append(tok)
            i += 1
        for oc in range(NCH):
            buf, res = stg[i % 2]
            tok = kb.emit("sy", lambda e, s=cm_h[oc * 128:(oc + 1) * 128, :], b=buf:
                          e.dma_start(out=b[:, 0:32], in_=s), waits=res.wr(),
                          inc=f"di{i % 2}", n=16)
            res.did_wr(tok)
            tok = kb.emit("ve", lambda e, b=buf: e.tensor_copy(
                out=xb[:, 0, 0:32], in_=b[:, 0:32]),
                waits=res.rd() + R["xb0"].wr(), inc="ve")
            res.did_rd(tok)
            R["xb0"].did_wr(tok)
            p1bf = p1[:].bitcast(BF16)
            tok = kb.emit("pe", lambda e, pb=p1bf: e.transpose(
                pb[0:32, 0:128], xb[:, 0, 0:32], ident[:]),
                waits=R["xb0"].rd() + R["ident"].rd() + R["p1"].wr(), inc="pe")
            R["xb0"].did_rd(tok)
            R["p1"].did_wr(tok)
            tok = kb.emit("ac", lambda e, oc=oc, pb=p1bf: e.activation(
                ct[:, oc * 128:(oc + 1) * 128], pb[0:32, 0:128], AF.Copy),
                waits=R["p1"].rd(), inc="ac")
            R["p1"].did_rd(tok)
            w_toks.append(tok)
            i += 1
        for src_h, dst in [(bq_h, bqc), (bk_h, bkc), (bv_h, bvc)]:
            for c in range(NCH):
                tok = kb.emit("sy", lambda e, s=src_h[c * 128:(c + 1) * 128], d=dst, c_=c:
                              e.dma_start(out=d[:, c_:c_ + 1],
                                          in_=s.rearrange("(p o) -> p o", o=1)),
                              inc="dc", n=16)
                w_toks.append(tok)
        tok = kb.emit("sy", lambda e: e.dma_start(out=bo_row, in_=bo_h.rearrange("(o f) -> o f", o=1)), inc="dc", n=16)
        w_toks.append(tok)
        tok = kb.emit("sy", lambda e: e.dma_start(out=bf_row, in_=bf_h.rearrange("(o f) -> o f", o=1)), inc="dc", n=16)
        w_toks.append(tok)
        rows_di = kb.cnt["dc"]
        bvb_tok = kb.emit("ve", lambda e: e.tensor_copy(out=bvb[:], in_=bvc[:]),
                          waits=[("dc", rows_di)], inc="ve")
        for half in range(2):
            bank = p0 if half == 0 else p1
            bres = R["p0"] if half == 0 else R["p1"]
            fns = [
                (lambda e, ci=ci, b=bank, h=half: e.matmul(
                    b[0:1, 0:512], bvb[:, ci:ci + 1],
                    wo[:, ci, h * 512:(h + 1) * 512],
                    start=(ci == 0), stop=(ci == NCH - 1)))
                for ci in range(NCH)]
            tok = kb.group("pe", fns,
                           waits=[bvb_tok] + w_toks[24:32] + bres.wr(), inc="pe")
            bres.did_wr(tok)
            tok = kb.emit("ve", lambda e, b=bank, h=half: e.tensor_tensor(
                out=bo_row[0:1, h * 512:(h + 1) * 512], in0=b[0:1, 0:512],
                in1=bo_row[0:1, h * 512:(h + 1) * 512], op=ALU.add),
                waits=bres.rd() + [("dc", rows_di)], inc="ve")
            bres.did_rd(tok)
            kb.emit("ve", lambda e, h=half: e.tensor_copy(
                out=bcomb[:, h * 512:(h + 1) * 512],
                in_=bo_row[0:1, h * 512:(h + 1) * 512]), inc="ve")
        bffn_tok = kb.emit("ve", lambda e: e.tensor_copy(out=bffn[:], in_=bf_row),
                           waits=[("dc", rows_di)], inc="ve")
        for row_bf, bc in [(bcomb, bcomb_bc), (bffn, bffn_bc)]:
            for half in range(2):
                bank = p0 if half == 0 else p1
                bres = R["p0"] if half == 0 else R["p1"]
                tok = kb.emit("pe", lambda e, b=bank, h=half, rb=row_bf: e.matmul(
                    b[:], ones_r[:], rb[:, h * 512:(h + 1) * 512],
                    start=True, stop=True),
                    waits=[("ve", kb.cnt["ve"])] + bres.wr() + R["ones"].rd(),
                    inc="pe")
                bres.did_wr(tok)
                tok = kb.emit("ve", lambda e, b=bank, h=half, d=bc: e.tensor_copy(
                    out=d[:, h * 512:(h + 1) * 512], in_=b[:]),
                    waits=bres.rd(), inc="ve")
                bres.did_rd(tok)
        setup_done = [("ve", kb.cnt["ve"]), ("ac", kb.cnt["ac"]),
                      ("pe", kb.cnt["pe"])]

        def load_x(st):
            par = st % 2
            res = R[f"x{par}"]
            tok = kb.emit("sy", lambda e, st=st, par=par: e.dma_start(
                out=x_sb[:, par, :, :],
                in_=x_h[st * STW * W:(st + 1) * STW * W, :].rearrange(
                    "(w p) f -> p w f", p=128)),
                waits=res.wr(), inc=f"dx{par}", n=16)
            res.did_wr(tok)

        load_x(0)

        def head_slice(buf, h, fr):
            return buf[64 * (h % 2):64 * (h % 2) + 64, h // 2, fr]

        def post_A1(gi):
            u1bf = u1[:].bitcast(BF16)
            fns = [(lambda e, c=c: e.transpose(
                u1bf[0:128, c * 128:(c + 1) * 128],
                ao[:, c * 128:(c + 1) * 128], ident[:])) for c in range(NCH)]
            tok = kb.group("pe", fns, waits=R["ao"].rd() + R["u1"].wr(), inc="pe")
            R["ao"].did_rd(tok)
            R["u1"].did_wr(tok)
            tok = kb.emit("ve", lambda e: e.tensor_copy(
                out=aofm[:], in_=u1bf[0:128, 0:1024].rearrange(
                    "p (c f) -> p c f", c=NCH)),
                waits=R["u1"].rd() + R["aofm"].wr(), inc="ve")
            R["u1"].did_rd(tok)
            R["aofm"].did_wr(tok)

        def post_A2(gi):
            for half in range(2):
                bank = p0 if half == 0 else p1
                bres = R["p0"] if half == 0 else R["p1"]
                fns = [(lambda e, ci=ci, b=bank, h=half: e.matmul(
                    b[:], aofm[:, ci, :], wo[:, ci, h * 512:(h + 1) * 512],
                    start=(ci == 0), stop=(ci == NCH - 1))) for ci in range(NCH)]
                tok = kb.group("pe", fns,
                               waits=R["aofm"].rd() + bres.wr(),
                               inc="pe")
                R["aofm"].did_rd(tok)
                bres.did_wr(tok)

        def post_X(gi):
            pw = gi % 2
            u0bf = u0[:].bitcast(BF16)
            x2b_r = R[f"x2b_{pw}"]
            x2fm_r = R[f"x2fm_{pw}"]
            fns = [(lambda e, c=c, pw=pw: e.transpose(
                u0bf[0:128, c * 128:(c + 1) * 128],
                x2b[:, pw, c * 128:(c + 1) * 128], ident[:])) for c in range(NCH)]
            tok = kb.group("pe", fns, waits=x2b_r.rd() + R["u0"].wr(), inc="pe")
            x2b_r.did_rd(tok)
            R["u0"].did_wr(tok)
            tok = kb.emit("ve", lambda e, pw=pw: e.tensor_copy(
                out=x2fm[:, pw], in_=u0bf[0:128, 0:1024].rearrange(
                    "p (c f) -> p c f", c=NCH)),
                waits=R["u0"].rd() + x2fm_r.wr(), inc="ve")
            R["u0"].did_rd(tok)
            x2fm_r.did_wr(tok)

        def post_XR(gi):
            pw = gi % 2
            bi = 0
            x2fm_r = R[f"x2fm_{pw}"]
            s0res = R[f"s{bi}"]
            fns = [(lambda e, ci=ci, pw=pw, bi=bi: e.matmul(
                ps[bi][0:32, 128:256], vr[:, ci, :], x2fm[:, pw, ci, :],
                start=(ci == 0), stop=(ci == NCH - 1))) for ci in range(NCH)]
            tok = kb.group("pe", fns, waits=x2fm_r.rd() + s0res.wr(), inc="pe")
            x2fm_r.did_rd(tok)
            s0res.did_wr(tok)
            tok = kb.emit("ve", lambda e, bi=bi: e.tensor_copy(
                out=xr[:], in_=ps[bi][0:32, 128:256]),
                waits=s0res.rd() + R["xr"].wr(), inc="ve")
            s0res.did_rd(tok)
            R["xr"].did_wr(tok)

        def pre_B(gi):
            wt_ = gi % STW
            par = (gi // STW) % 2
            tok = kb.emit("gp", lambda e, w_=wt_, p_=par: e.tensor_tensor(
                out=y1[:], in0=x_sb[:, p_, w_, :], in1=bcomb_bc[:], op=ALU.add),
                waits=R["y1"].wr() + R[f"x{par}"].rd() + setup_done, inc="gp")
            R[f"x{par}"].did_rd(tok)
            R["y1"].did_wr(tok)

        def post_B1(gi):
            y1pre = R["y1"].rd()
            for half in range(2):
                bank = p0 if half == 0 else p1
                bres = R["p0"] if half == 0 else R["p1"]
                csl = C_SUM0 + half
                tok = kb.emit("ve", lambda e, b=bank, h=half, cs=csl:
                              e.scalar_tensor_tensor(
                                  out=y1[:, h * 512:(h + 1) * 512], in0=b[:],
                                  scalar=1.0,
                                  in1=y1[:, h * 512:(h + 1) * 512],
                                  op0=ALU.mult, op1=ALU.add,
                                  accum_out=cols[:, cs:cs + 1]),
                              waits=bres.rd() + y1pre + R["cols1"].wr(), inc="ve")
                bres.did_rd(tok)
                if half == 1:
                    R["y1"].did_wr(tok)
                    R["cols1"].did_wr(tok)

        def post_B2(gi):
            pw = gi % 2
            _ln(R["y1"], y1, x2[:, pw, :], x2b[:, pw, :], True,
                bf_res=R[f"x2b_{pw}"], f32_res=R[f"x2_{pw}"])

        def post_CORR(gi):
            for half in range(2):
                bank = p0 if half == 0 else p1
                bres = R["p0"] if half == 0 else R["p1"]
                tok = kb.emit("pe", lambda e, b=bank, h=half: e.matmul(
                    b[:], xr[:], ct[:, h * 512:(h + 1) * 512],
                    start=True, stop=False),
                    waits=R["xr"].rd() + bres.wr(), inc="pe")
                R["xr"].did_rd(tok)
                bres.did_wr(tok)

        def post_C1(gi):
            pw = gi % 2
            x2fm_r = R[f"x2fm_{pw}"]
            for half in range(2):
                bank = p0 if half == 0 else p1
                bres = R["p0"] if half == 0 else R["p1"]
                fns = [(lambda e, ci=ci, b=bank, h=half, pw=pw: e.matmul(
                    b[:], x2fm[:, pw, ci, :], wt[:, ci, h * 512:(h + 1) * 512],
                    start=False, stop=(ci == NCH - 1))) for ci in range(NCH)]
                tok = kb.group("pe", fns,
                               waits=x2fm_r.rd() + bres.wr(),
                               inc="pe")
                x2fm_r.did_rd(tok)
                bres.did_wr(tok)

        def pre_C2(gi):
            pw = gi % 2
            tok = kb.emit("gp", lambda e, pw=pw: e.tensor_tensor(
                out=y2[:], in0=x2[:, pw, :], in1=bffn_bc[:], op=ALU.add),
                waits=R["y2"].wr() + R[f"x2_{pw}"].rd() + setup_done, inc="gp")
            R[f"x2_{pw}"].did_rd(tok)
            R["y2"].did_wr(tok)

        def post_C2(gi):
            opar = gi % 2
            pw = gi % 2
            y2pre = R["y2"].rd()
            for half in range(2):
                bank = p0 if half == 0 else p1
                bres = R["p0"] if half == 0 else R["p1"]
                csl = C_SUM0 + half
                tok = kb.emit("ve", lambda e, b=bank, h=half, cs=csl:
                              e.scalar_tensor_tensor(
                                  out=y2[:, h * 512:(h + 1) * 512], in0=b[:],
                                  scalar=1.0, in1=y2[:, h * 512:(h + 1) * 512],
                                  op0=ALU.mult, op1=ALU.add,
                                  accum_out=cols[:, cs:cs + 1]),
                              waits=bres.rd() + y2pre + R["cols1"].wr(), inc="ve")
                bres.did_rd(tok)
                if half == 1:
                    R["y2"].did_wr(tok)
                    R["cols1"].did_wr(tok)
            ores = R["out0"]
            _ln(R["y2"], y2, outb[:], None, False, out_res=ores)
            tok = kb.emit("gp", lambda e, gi=gi, op=opar: e.dma_start(
                out=out_h[gi * W:(gi + 1) * W, :], in_=outb[:]),
                waits=ores.rd(), inc=f"do{opar}", n=16)
            ores.did_rd(tok)

        def _ln(yres, ybuf, out_f32, out_bf, also_bf, out_res=None,
                bf_res=None, f32_res=None):
            if not isinstance(out_f32, bass.AP):
                out_f32 = out_f32[:]
            if out_bf is not None and not isinstance(out_bf, bass.AP):
                out_bf = out_bf[:]
            tok = kb.emit("ve", lambda e: e.tensor_tensor(
                out=cols[:, C_SUM:C_SUM + 1], in0=cols[:, C_SUM0:C_SUM0 + 1],
                in1=cols[:, C_SUM1:C_SUM1 + 1], op=ALU.add),
                waits=R["cols1"].rd(), inc="ve")
            tok = kb.emit("ac", lambda e: e.activation(
                aofm[:].rearrange("p c f -> p (c f)"), ybuf[:], AF.Square,
                accum_out=cols[:, C_VAR:C_VAR + 1]),
                waits=yres.rd() + R["aofm"].wr() + R["cols2"].wr(), inc="ac")
            yres.did_rd(tok)
            R["aofm"].did_wr(tok)
            sq_tok = tok
            kb.emit("ve", lambda e: e.tensor_tensor(
                out=cols[:, C_M2:C_M2 + 1], in0=cols[:, C_SUM:C_SUM + 1],
                in1=cols[:, C_SUM:C_SUM + 1], op=ALU.mult), inc="ve")
            tok = kb.emit("ve", lambda e: e.scalar_tensor_tensor(
                out=cols[:, C_VAR:C_VAR + 1], in0=cols[:, C_M2:C_M2 + 1],
                scalar=-1.0 / D, in1=cols[:, C_VAR:C_VAR + 1],
                op0=ALU.mult, op1=ALU.add),
                waits=[sq_tok], inc="ve")
            tok = kb.emit("ac", lambda e: e.activation(
                cols[:, C_LNV:C_LNV + 1], cols[:, C_VAR:C_VAR + 1], AF.Ln,
                bias=eps_c[:], scale=1.0 / D),
                waits=[tok] + R["ones"].rd(), inc="ac")
            rstd_tok = kb.emit("ac", lambda e: e.activation(
                cols[:, C_RSTD:C_RSTD + 1], cols[:, C_LNV:C_LNV + 1], AF.Exp,
                scale=-0.5), inc="ac")
            nmr_tok = kb.emit("ve", lambda e: e.tensor_scalar(
                out=cols[:, C_NMR:C_NMR + 1], in0=cols[:, C_SUM:C_SUM + 1],
                scalar1=cols[:, C_RSTD:C_RSTD + 1], scalar2=-1.0 / D,
                op0=ALU.mult, op1=ALU.mult), waits=[rstd_tok], inc="ve")
            R["cols2"].did_wr(nmr_tok)
            if also_bf:
                tok = kb.emit("ac", lambda e, o=out_bf: e.activation(
                    o, ybuf[:], AF.Identity, bias=cols[:, C_NMR:C_NMR + 1],
                    scale=cols[:, C_RSTD:C_RSTD + 1]),
                    waits=[nmr_tok] + yres.rd() + bf_res.wr(), inc="ac")
                yres.did_rd(tok)
                bf_res.did_wr(tok)
            w_extra = out_res.wr() if out_res is not None else []
            if f32_res is not None:
                w_extra = w_extra + f32_res.wr()
            tok = kb.emit("ac", lambda e, o=out_f32: e.activation(
                o, ybuf[:], AF.Identity, bias=cols[:, C_NMR:C_NMR + 1],
                scale=cols[:, C_RSTD:C_RSTD + 1]),
                waits=[nmr_tok] + yres.rd() + w_extra, inc="ac")
            yres.did_rd(tok)
            if out_res is not None:
                out_res.did_wr(tok)
            if f32_res is not None:
                f32_res.did_wr(tok)
            if also_bf:
                R["cols2"].did_rd(tok)

        def cast_xb(stc):
            parc = stc % 2
            for w2 in range(STW):
                xres = R[f"xb{w2}"]
                tok = kb.emit("gp", lambda e, w2=w2, p_=parc:
                              e.tensor_copy(out=xb[:, w2, :],
                                            in_=x_sb[:, p_, w2, :]),
                              waits=R[f"x{parc}"].rd() + xres.wr(), inc="gp")
                R[f"x{parc}"].did_rd(tok)
                xres.did_wr(tok)
                xw = R["xfm"].wr() if w2 == 0 else []
                for c in range(NCH):
                    tok = kb.emit("sy", lambda e, w2=w2, c=c: e.dma_start_transpose(
                        xfm[:, c, w2 * W:(w2 + 1) * W],
                        xb[:, w2, c * 128:(c + 1) * 128]),
                        waits=xres.rd() + xw, inc="dt0", n=16)
                    xw = []
            batch_end = ("dt0", kb.cnt["dt0"])
            for w2 in range(STW):
                R[f"xb{w2}"].did_rd(batch_end)
            R["xfm"].did_wr(batch_end)

        for gi in (range(NW + 2) if PHASE >= 1 else []):
            att = gi < NW
            st, wt_ = gi // STW, gi % STW
            par = st % 2
            first = (gi % WPS == 0)

            if gi - 1 >= 0 and gi - 1 < NW and PHASE >= 4:
                post_A1(gi - 1)
                pre_B(gi - 1)
            if gi - 2 >= 0 and PHASE >= 4:
                post_X(gi - 2)
                pre_C2(gi - 2)

            if att and wt_ == 0:
                if st == 0:
                    cast_xb(0)
                if st > 0:
                    tok = kb.emit("gp", lambda e: e.tensor_copy(
                        out=kprev[:], in_=kfm[:, :, 3 * W:4 * W]),
                        waits=R["kfm"].wr() + R["kprev"].wr(), inc="gp")
                    R["kprev"].did_wr(tok)
                    R["kfm"].did_rd(tok)
                    tok = kb.emit("gp", lambda e: e.tensor_copy(
                        out=vprev[:], in_=v_sb[:, 3, :]),
                        waits=R["v"].wr() + R["vprev"].wr(), inc="gp")
                    R["vprev"].did_wr(tok)
                    R["v"].did_rd(tok)

                qkv_banks = [(p0, R["p0"]), (p1, R["p1"]),
                             (ps[2], R["s2"]), (ps[3], R["s3"])]
                for dst, wmat, bias, dres in [(qfm, wq, bqc, R["qfm"]),
                                              (kfm, wk, bkc, R["kfm"])]:
                    for co in range(NCH):
                        bank, bres = qkv_banks[co % 4]
                        fns = [(lambda e, ci=ci, b=bank, co=co, wm=wmat: e.matmul(
                            b[:], wm[:, ci, co * 128:(co + 1) * 128],
                            xfm[:, ci, :], start=(ci == 0), stop=(ci == NCH - 1)))
                            for ci in range(NCH)]
                        tok = kb.group("pe", fns,
                                       waits=R["xfm"].rd() + bres.wr()
                                       + (setup_done if gi == 0 else []), inc="pe")
                        R["xfm"].did_rd(tok)
                        bres.did_wr(tok)
                        tok = kb.emit("ve", lambda e, b=bank, co=co, d=dst, bi=bias:
                                      e.tensor_scalar(
                                          out=d[:, co, :], in0=b[:],
                                          scalar1=bi[:, co:co + 1], scalar2=None,
                                          op0=ALU.add),
                                      waits=bres.rd()
                                      + (dres.wr() if co == 0 else []), inc="ve")
                        bres.did_rd(tok)
                        if co == NCH - 1:
                            dres.did_wr(tok)
                for half in range(2):
                    for tc in range(STW):
                        bank, bres = qkv_banks[(half * STW + tc) % 4]
                        fns = [(lambda e, ci=ci, b=bank, tc=tc, h=half: e.matmul(
                            b[:], xfm[:, ci, tc * W:(tc + 1) * W],
                            wv[:, ci, h * 512:(h + 1) * 512],
                            start=(ci == 0), stop=(ci == NCH - 1)))
                            for ci in range(NCH)]
                        tok = kb.group("pe", fns, waits=R["xfm"].rd() + bres.wr(),
                                       inc="pe")
                        R["xfm"].did_rd(tok)
                        bres.did_wr(tok)
                        tok = kb.emit("ac", lambda e, b=bank, tc=tc, h=half:
                                      e.copy(v_sb[:, tc, h * 512:(h + 1) * 512],
                                             b[:]),
                                      waits=bres.rd()
                                      + (R["v"].wr() if (half == 0 and tc == 0)
                                         else []), inc="ac")
                        bres.did_rd(tok)
                        if half == 1 and tc == STW - 1:
                            R["v"].did_wr(tok)

            if att and wt_ == 1 and st + 1 < NST:
                load_x(st + 1)
                cast_xb(st + 1)

            for g in (range(8) if att and PHASE >= 2 else []):
                bank, bres = ps[g % 4], R[f"s{g % 4}"]
                fns = []
                for hh in range(2):
                    h = g + 8 * hh
                    q_sl = head_slice(qfm, h, slice(wt_ * W, (wt_ + 1) * W))
                    jts = [] if first else [0]
                    jts.append(1)
                    for jt in jts:
                        if jt == 0:
                            if wt_ == 0:
                                k_sl = head_slice(kprev, h, slice(0, W))
                            else:
                                k_sl = head_slice(kfm, h, slice((wt_ - 1) * W, wt_ * W))
                        else:
                            k_sl = head_slice(kfm, h, slice(wt_ * W, (wt_ + 1) * W))
                        reg = (2 * hh + jt) * 128
                        fns.append(lambda e, b=bank, r=reg, ks=k_sl, qs=q_sl:
                                   e.matmul(b[:, r:r + 128], ks, qs,
                                            start=True, stop=True))
                w_ = (bres.wr() + R["qfm"].rd()
                      + (R["kprev"].rd() if (wt_ == 0 and not first) else [])
                      + R["kfm"].rd())
                tok = kb.group("pe", fns, waits=w_, inc="pe")
                bres.did_wr(tok)
                R["qfm"].did_rd(tok)
                R["kfm"].did_rd(tok)
                if wt_ == 0 and not first:
                    R["kprev"].did_rd(tok)
                eres = R[f"ep{g}"]
                if SUB < 1:
                    continue
                if first:
                    tok = kb.emit("ac", lambda e, b=bank, g=g: e.activation(
                        ep[:, g, :].rearrange("p (hh x) -> p hh x", hh=2)[:, :, 128:256],
                        b[:].rearrange("p (hh x) -> p hh x", hh=2)[:, :, 128:256],
                        AF.Exp, scale=0.125),
                        waits=bres.rd() + eres.wr(), inc="ac")
                else:
                    tok = kb.emit("ac", lambda e, b=bank, g=g: e.activation(
                        ep[:, g, :], b[:], AF.Exp, scale=0.125),
                        waits=bres.rd() + eres.wr(), inc="ac")
                bres.did_rd(tok)
                eres.did_wr(tok)

            if gi - 2 >= 0 and PHASE >= 4:
                post_XR(gi - 2)
            if gi - 1 >= 0 and gi - 1 < NW and PHASE >= 4:
                post_A2(gi - 1)

            if att and PHASE >= 3 and SUB >= 2:
                ep_cur = ep[:].rearrange("p g (hhjt x) -> p (g hhjt) x",
                                         x=128)[:, 1::2, :]
                mask_waits = []
                for g in range(8):
                    mask_waits += R[f"ep{g}"].rd()
                tok = kb.emit("ve", lambda e, epc=ep_cur: e.tensor_tensor(
                    out=epc, in0=epc,
                    in1=m01[:].unsqueeze(1).broadcast_to([128, 16, 128]),
                    op=ALU.mult),
                    waits=mask_waits + R["m01"].rd(), inc="ve")
                for g in range(8):
                    R[f"ep{g}"].did_wr(tok)

            if att and PHASE >= 3:
                fns = []
                sum_waits = R["u0"].wr() + R["ones"].rd()
                for h in range(H):
                    g, hh = h % 8, h // 8
                    jts = [1] if first else [0, 1]
                    for jt in jts:
                        reg = (2 * hh + jt) * 128
                        st_ = (len(jts) == 1) or (jt == jts[0])
                        sp_ = (len(jts) == 1) or (jt == jts[-1])
                        fns.append(lambda e, g=g, r=reg, h=h, st=st_, sp=sp_:
                                   e.matmul(u0[:, h:h + 1], ep[:, g, r:r + 128],
                                            ones_c[:], start=st, stop=sp))
                for g in range(8):
                    sum_waits += R[f"ep{g}"].rd()
                tok = kb.group("pe", fns, waits=sum_waits, inc="pe")
                R["u0"].did_wr(tok)
                for g in range(8):
                    R[f"ep{g}"].did_rd(tok)
                tok = kb.emit("ve", lambda e: e.reciprocal(out=r16[:],
                                                           in_=u0[:, 0:16]),
                              waits=R["u0"].rd() + R["r16"].wr(), inc="ve")
                R["u0"].did_rd(tok)
                R["r16"].did_wr(tok)

            if gi - 1 >= 0 and gi - 1 < NW and PHASE >= 4:
                post_B1(gi - 1)
                post_B2(gi - 1)

            if att and PHASE >= 3:
                for bank_i in range(2):
                    bank = u0 if bank_i == 0 else u1
                    bres = R["u0"] if bank_i == 0 else R["u1"]
                    fns = []
                    uw = bres.wr() + R["v"].rd() + (
                        R["vprev"].rd() if (wt_ == 0 and not first) else [])
                    for h8 in range(8):
                        h = bank_i * 8 + h8
                        g, hh = h % 8, h // 8
                        jts = [1] if first else [0, 1]
                        for jt in jts:
                            reg = (2 * hh + jt) * 128
                            if jt == 0:
                                if wt_ == 0:
                                    v_sl = vprev[:, h * 64:(h + 1) * 64]
                                else:
                                    v_sl = v_sb[:, wt_ - 1, h * 64:(h + 1) * 64]
                            else:
                                v_sl = v_sb[:, wt_, h * 64:(h + 1) * 64]
                            st_ = (len(jts) == 1) or (jt == jts[0])
                            sp_ = (len(jts) == 1) or (jt == jts[-1])
                            fns.append(lambda e, b=bank, g=g, r=reg, vs=v_sl,
                                       h8=h8, st=st_, sp=sp_:
                                       e.matmul(b[:, h8 * 64:(h8 + 1) * 64],
                                                ep[:, g, r:r + 128], vs,
                                                start=st, stop=sp))
                    tok = kb.group("pe", fns, waits=uw, inc="pe")
                    bres.did_wr(tok)
                    R["v"].did_rd(tok)
                    if wt_ == 0 and not first:
                        R["vprev"].did_rd(tok)

            if gi - 2 >= 0 and PHASE >= 4:
                post_CORR(gi - 2)
                post_C1(gi - 2)

            if att and PHASE >= 3:
                for bank_i in range(2):
                    bank = u0 if bank_i == 0 else u1
                    bres = R["u0"] if bank_i == 0 else R["u1"]
                    tok = kb.emit("ve", lambda e, b=bank, bi=bank_i: e.tensor_tensor(
                        out=ao[:, bi * 512:(bi + 1) * 512].rearrange(
                            "p (h f) -> p h f", h=8),
                        in0=b[:].rearrange("p (h f) -> p h f", h=8),
                        in1=r16[:, bi * 8:(bi + 1) * 8].unsqueeze(2).broadcast_to(
                            [128, 8, 64]),
                        op=ALU.mult),
                        waits=bres.rd() + R["r16"].rd()
                        + (R["ao"].wr() if bank_i == 0 else []), inc="ve")
                    bres.did_rd(tok)
                    R["r16"].did_rd(tok)
                    if bank_i == 1:
                        R["ao"].did_wr(tok)

            if gi - 2 >= 0 and PHASE >= 4:
                post_C2(gi - 2)

        kb.emit("gp", lambda e: e.nop(),
                waits=[("do0", kb.cnt["do0"]), ("do1", kb.cnt["do1"])])

        def run(engine_obj, eng_name):
            for fn, ws, inc_sem, inc_n in kb.ops[eng_name]:
                for sname, val in ws:
                    engine_obj.wait_ge(sems[sname], val)
                inst = fn(engine_obj)
                if inc_sem:
                    inst.then_inc(sems[inc_sem], inc_n)

        @block.sync
        def _(e):
            run(e, "sy")

        @block.tensor
        def _(e):
            run(e, "pe")

        @block.scalar
        def _(e):
            run(e, "ac")

        @block.vector
        def _(e):
            run(e, "ve")

        @block.gpsimd
        def _(e):
            run(e, "gp")

    return nc


def kernel(L0_attn_x, L0_attn_Wq, L0_attn_bq, L0_attn_Wk, L0_attn_bk,
           L0_attn_Wv, L0_attn_bv, L0_attn_Wo, L0_attn_bo,
           L0_ffn_W, L0_ffn_b, L0_ffn_V_r, L0_ffn_C):
    x = np.ascontiguousarray(np.asarray(L0_attn_x, dtype=np.float32))
    B, SL, DM = x.shape
    n_cores = 8
    bpc = B // n_cores
    nw = bpc * SL // W

    nc = build_program(nw)

    common = {
        "Wq": np.ascontiguousarray(L0_attn_Wq, np.float32),
        "Wk": np.ascontiguousarray(L0_attn_Wk, np.float32),
        "Wv": np.ascontiguousarray(L0_attn_Wv, np.float32),
        "Wo": np.ascontiguousarray(L0_attn_Wo, np.float32),
        "Wf": np.ascontiguousarray(L0_ffn_W, np.float32),
        "bq": np.ascontiguousarray(L0_attn_bq, np.float32),
        "bk": np.ascontiguousarray(L0_attn_bk, np.float32),
        "bv": np.ascontiguousarray(L0_attn_bv, np.float32),
        "bo": np.ascontiguousarray(L0_attn_bo, np.float32),
        "bf": np.ascontiguousarray(L0_ffn_b, np.float32),
        "Vr": np.ascontiguousarray(L0_ffn_V_r, np.float32),
        "Cm": np.ascontiguousarray(L0_ffn_C, np.float32),
    }
    in_maps = []
    for c in range(n_cores):
        shard = np.ascontiguousarray(
            x[c * bpc:(c + 1) * bpc].reshape(bpc * SL, DM))
        in_maps.append(dict(common, x=shard))

    res = run_bass_kernel_spmd(nc, in_maps, core_ids=list(range(n_cores)))
    out = np.concatenate(
        [r["out"].reshape(bpc, SL, DM) for r in res.results], axis=0)
    return out.astype(np.float32)
